# version 02
# baseline (speedup 1.0000x reference)
import os
import numpy as np
import ml_dtypes
from contextlib import ExitStack

import concourse.bass as bass
import concourse.mybir as mybir
from concourse.bass_utils import run_bass_kernel_spmd

F32 = mybir.dt.float32
BF16 = mybir.dt.bfloat16
AF = mybir.ActivationFunctionType
ALU = mybir.AluOpType
AX = mybir.AxisListType

C_RNE = 12582912.0
NEG_BIG = -1.0e30

S, D = 2048, 1024
NT = S // 128
ND = D // 128
DK = 64
DH = 4096
NE, EF = 32, 512
L1_SCALE = (1.0 / 1024.0) * (1.0 / 8.0)
L1_OUT_SCALE = (1.0 / 32.0) * (1.0 / 32.0)

STAGE = int(os.environ.get("KMK_STAGE", "99"))
DBG = os.environ.get("KMK_DBG", "") == "1"
_CACHE = {}


def _chunks(c0, c1, width=512):
    out = []
    c = c0
    while c < c1:
        n = min(c1, (c // width + 1) * width)
        out.append((c, n))
        c = n
    return out


class Sched:
    ENGINES = ("Y", "G", "V", "S", "T")

    def __init__(self):
        self.ops = []
        self.idx = {e: 0 for e in self.ENGINES}
        self.last_w = {}
        self.readers = {}
        self.deps = []
        self.final_waits = []
        self.pending_fence = None
        self.tok_dma = {}

    def op(self, engine, fn, r=(), w=(), dma=False, tdrain=False):
        if engine in ("V", "G") and not dma and self.idx[engine] > 0:
            prev = (engine, self.idx[engine] - 1)
            if any(self.last_w.get(res) == prev for res in list(r) + list(w)):
                self._raw_op(engine, lambda e: e.drain(), dma=False)
        tok = (engine, self.idx[engine])
        self.idx[engine] += 1
        dep = set()
        if self.pending_fence is not None:
            done = self.pending_fence.pop(engine, None)
            if done:
                dep.update(done)
            if not self.pending_fence:
                self.pending_fence = None
        for res in r:
            lw = self.last_w.get(res)
            if lw is not None and (lw[0] != engine or self.tok_dma.get(lw)):
                dep.add(lw)
        for res in w:
            lw = self.last_w.get(res)
            if lw is not None and (lw[0] != engine or self.tok_dma.get(lw)):
                dep.add(lw)
            for rd in self.readers.get(res, ()):
                if rd[0] != engine or self.tok_dma.get(rd):
                    dep.add(rd)
        for res in r:
            self.readers.setdefault(res, []).append(tok)
        for res in w:
            self.last_w[res] = tok
            self.readers[res] = []
        dep.discard(tok)
        self.ops.append((engine, fn, dma, tdrain))
        self.deps.append(sorted(dep))
        self.tok_dma[tok] = dma
        return tok

    def _raw_op(self, engine, fn, dma):
        self.idx[engine] += 1
        self.ops.append((engine, fn, dma, False))
        self.deps.append([])

    def fence(self):
        last = {}
        for e in self.ENGINES:
            if self.idx[e]:
                last[e] = (e, self.idx[e] - 1)
        pf = {}
        for e in self.ENGINES:
            pf[e] = [t for pe, t in last.items() if pe != e]
        self.pending_fence = pf

    def Y(self, fn, r=(), w=()):
        return self.op("Y", fn, r, w, dma=True)

    def G(self, fn, r=(), w=(), dma=True):
        return self.op("G", fn, r, w, dma=dma)

    def V(self, fn, r=(), w=()):
        return self.op("V", fn, r, w)

    def Sc(self, fn, r=(), w=()):
        return self.op("S", fn, r, w)

    def Tn(self, fn, r=(), w=(), tdrain=False):
        return self.op("T", fn, r, w, tdrain=tdrain)

    def final_wait(self, engine, token):
        self.final_waits.append((engine, token))

    def finalize(self, nc, es):
        NPOOL = {"Y": 24, "G": 12, "V": 2, "S": 2, "T": 2}
        per_engine = {e: [] for e in self.ENGINES}
        for k, op_ in enumerate(self.ops):
            per_engine[op_[0]].append(k)
        pool = {}
        dma_sem = {}
        pool_cnt = {}
        for e in self.ENGINES:
            for li, k in enumerate(per_engine[e]):
                if self.ops[k][2]:
                    npool = NPOOL.get(e, 2)
                    pk = (e, (sum(1 for kk in per_engine[e][:li + 1] if self.ops[kk][2]) - 1) % npool)
                    pool_cnt[pk] = pool_cnt.get(pk, 0) + 1
                    dma_sem[(e, li)] = (pk, pool_cnt[pk])
        needed = {e: set() for e in self.ENGINES}
        all_deps = list(self.deps)
        for dep in all_deps:
            for (pe, pi) in dep:
                if (pe, pi) not in dma_sem:
                    needed[pe].add(pi)
        for (_, tok) in self.final_waits:
            if tok not in dma_sem:
                needed[tok[0]].add(tok[1])
        inc_amt = {e: {} for e in self.ENGINES}
        cum = {}
        for e in self.ENGINES:
            c = 0
            arr = [0] * (self.idx[e] + 1)
            for li, k in enumerate(per_engine[e]):
                if li in needed[e] and not self.ops[k][2]:
                    inc_amt[e][li] = 1
                    c += 1
                arr[li + 1] = c
            cum[e] = arr
        sems = {e: es.enter_context(nc.semaphore(f"q{e}")) for e in self.ENGINES}
        psems = {}
        for pk in sorted(set(pk for pk, _ in dma_sem.values())):
            psems[pk] = es.enter_context(nc.semaphore(f"d{pk[0]}{pk[1]}"))
        watermark = {e: {} for e in self.ENGINES}
        fw = {}
        for (we, tok) in self.final_waits:
            fw.setdefault(we, []).append(tok)

        def wait_for(eng, e, tok):
            if tok in dma_sem:
                pk, cnt = dma_sem[tok]
                val = 16 * cnt
                key = ("p", pk)
            else:
                pe, pi = tok
                val = cum[pe][pi + 1]
                if val == 0:
                    return
                key = ("e", pe)
            if val > watermark[e].get(key, 0):
                eng.wait_ge(psems[key[1]] if key[0] == "p" else sems[key[1]], val)
                watermark[e][key] = val

        def body_for(e):
            def body(eng):
                for li, k in enumerate(per_engine[e]):
                    _, fn, dma, tdrain = self.ops[k]
                    tok = (e, li)
                    if dma:
                        pk, cnt = dma_sem[tok]
                        if cnt > 1:
                            val = 16 * (cnt - 1)
                            key = ("p", pk)
                            if val > watermark[e].get(key, 0):
                                eng.wait_ge(psems[pk], val)
                                watermark[e][key] = val
                    for dtok in self.deps[k]:
                        wait_for(eng, e, dtok)
                    inst = fn(eng)
                    if dma:
                        pk, cnt = dma_sem[tok]
                        inst.then_inc(psems[pk], 16)
                    elif li in inc_amt[e]:
                        if e in ("V", "G", "S") or tdrain:
                            eng.drain().then_inc(sems[e], inc_amt[e][li])
                        else:
                            inst.then_inc(sems[e], inc_amt[e][li])
                for tok in fw.get(e, ()):
                    wait_for(eng, e, tok)
            return body

        blk = es.enter_context(nc.Block())
        names = {"Y": "sync", "G": "gpsimd", "V": "vector", "S": "scalar", "T": "tensor"}
        for e in self.ENGINES:
            if self.idx[e] or e in fw:
                getattr(blk, names[e])(body_for(e))


class Kern:
    def __init__(self, nc, es, di, ds, out):
        self.nc, self.es, self.di, self.ds, self.out = nc, es, di, ds, out
        self.sch = Sched()
        self.PS = [es.enter_context(nc.psum_tensor(f"ps{i}", [128, 1024], F32)) for i in range(4)]
        self.PSb = [p[:].bitcast(BF16) for p in self.PS]
        self.PR = [f"ps{i}" for i in range(4)]

        def T_(n, s, d=F32):
            return es.enter_context(nc.sbuf_tensor(n, s, d))

        self.T_ = T_
        self.x_sb = T_("x_sb", [128, NT * D])
        self.ident = T_("ident", [128, 128], BF16)
        self.mst = T_("mst", [128, 128])
        self.mts = T_("mts", [128, 128])
        self.smsc = T_("smsc", [128, 1])
        self.convs = T_("convs", [128, 128])
        self.junk = T_("junk", [128, 1024], BF16)
        self.onesf = T_("onesf", [1, 128])
        self.onesb = T_("onesb", [1, 128], BF16)
        self.identf = T_("identf", [128, 128])
        self.ln_m = T_("ln_m", [128, NT])
        self.ln_v = T_("ln_v", [128, NT])
        self.ln_r = T_("ln_r", [128, NT])
        self.qsc = T_("qsc", [128, 4])
        self.qb = T_("qb", [128, 4], BF16)
        self.dbg = T_("dbg", [128, 64])
        self.rq = T_("rq", [128, NT])
        self.nmt = T_("nmt", [128, 2 * NT])
        self.rt = T_("rt", [128, 2 * NT])
        self.lr_t = T_("lr_t", [128, NT])
        self.augf = T_("augf", [128, NT])
        self.aug2 = T_("aug2", [128, NT])
        self.augb = T_("augb", [128, 2 * NT], BF16)
        self.augs = T_("augs", [32, 128], BF16)
        self.tmpc = T_("tmpc", [128, 4])
        self.wsc = T_("wsc", [128, 2])
        self.wsc2 = T_("wsc2", [128, 2])
        self.xn_t = [T_(f"xn{i}", [128, 1024]) for i in range(2)]
        self.xnb = T_("xnb", [128, 1024], BF16)
        self.sqt = T_("sqt", [128, 1024])
        self.XT = [f"x{t}" for t in range(NT)]

    def emit_loads(self):
        sch, di = self.sch, self.di
        x_view = di["x"].rearrange("(a p) c -> p a c", p=128)
        sch.Y(lambda e: e.dma_start(self.x_sb[:].rearrange("p (a c) -> p a c", a=NT), x_view),
              w=self.XT)
        sch.Y(lambda e: e.dma_start(self.ident[:], di["ident_b"]), w=["ident"])
        sch.Y(lambda e: e.dma_start(self.mst[:], di["mask_st"]), w=["mst"])
        sch.Y(lambda e: e.dma_start(self.mts[:], di["mask_ts"]), w=["mts"])
        sch.Y(lambda e: e.dma_start(self.smsc[:], di["sm_scale"]), w=["smsc"])
        sch.Y(lambda e: e.dma_start(self.convs[:], di["convp"]), w=["convs"])
        sch.V(lambda e: e.memset(self.onesf[:], 1.0), w=["onesf"])
        sch.V(lambda e: e.memset(self.onesb[:], 1.0), w=["onesb"])
        sch.V(lambda e: e.tensor_copy(self.identf[:], self.ident[:]), r=["ident"], w=["identf"])
        sch.V(lambda e: e.memset(self.junk[:], 0.0), w=["junk"])

    def col2row(self, src_ap, n, r0, rres, wres):
        sch, ds = self.sch, self.ds
        sch.Tn(lambda e: e.transpose(self.PSb[3][0:n, 0:128], src_ap, self.ident[:]),
               r=list(rres) + ["ident", "ps3"], w=["ps3"], tdrain=True)
        sch.V(lambda e: e.tensor_copy(self.augs[0:n, :], self.PSb[3][0:n, 0:128]),
              r=["ps3", "augs"], w=["augs"])
        sch.Y(lambda e: e.dma_start(ds["row_d"][r0:r0 + n, 0:128], self.augs[0:n, :]),
              r=["augs"], w=[wres])

    def emit_wprep(self):
        sch, di, ds, T_ = self.sch, self.di, self.ds, self.T_
        wst = [T_(f"wst{i}", [128, 2048]) for i in range(2)]
        wqb = T_("wqb", [128, 2048], BF16)
        wacc = T_("wacc", [128, 2])
        wtmp = T_("wtmp", [128, 2])

        def tiles(which):
            if which == "w1":
                for i in range(16):
                    dc, half = i // 2, i % 2
                    yield i, di["w1T"][dc * 128:(dc + 1) * 128,
                                       half * 2048:half * 2048 + 2048], 2048, (dc, half)
            else:
                for i in range(32):
                    yield i, di["w2T"][i * 128:(i + 1) * 128, :], 1024, i

        for col, which in ((0, "w1"), (1, "w2")):
            sch.V(lambda e, col=col: e.memset(wacc[:, col:col + 1], 0.0), w=[f"wacc{col}"])
            for i, src, wid, _ in tiles(which):
                b = i % 2
                sch.Y(lambda e, s=src, b=b, wid=wid: e.dma_start(wst[b][:, 0:wid], s),
                      r=[f"wst{b}"], w=[f"wst{b}"])
                sch.V(lambda e, b=b, wid=wid, col=col: e.reduce_sum(
                    wtmp[:, col:col + 1], wst[b][:, 0:wid], axis=AX.X,
                    apply_absolute_value=True), r=[f"wst{b}"], w=[f"wtmp{col}"])
                sch.V(lambda e, col=col: e.tensor_add(
                    wacc[:, col:col + 1], wacc[:, col:col + 1], wtmp[:, col:col + 1]),
                    r=[f"wtmp{col}"], w=[f"wacc{col}"])
            sch.Tn(lambda e, col=col: e.transpose(
                self.PS[3][0:1, 0:128], wacc[:, col:col + 1], self.identf[:]),
                r=[f"wacc{col}", "identf", "ps3"], w=["ps3"], tdrain=True)
            sch.V(lambda e: e.reduce_sum(self.tmpc[0:1, 0:1], self.PS[3][0:1, 0:128], axis=AX.X),
                  r=["ps3"], w=["wtot"])
            sch.V(lambda e: e.tensor_scalar(
                self.tmpc[0:1, 1:2], self.tmpc[0:1, 0:1], 1.0 / (D * DH), 1e-5,
                op0=ALU.mult, op1=ALU.max), r=["wtot"], w=["wmean"])
            sch.V(lambda e: e.reciprocal(self.tmpc[0:1, 2:3], self.tmpc[0:1, 1:2]),
                  r=["wmean"], w=["wrecip"])
            sch.Tn(lambda e: e.matmul(self.PS[3][:, 0:1], self.onesf[:], self.tmpc[0:1, 2:3],
                                      start=True, stop=True),
                   r=["wrecip", "onesf", "ps3"], w=["ps3"], tdrain=True)
            sch.Sc(lambda e, col=col: e.copy(self.wsc[:, col:col + 1], self.PS[3][:, 0:1]),
                   r=["ps3"], w=[f"wsc{col}"])
            sch.Tn(lambda e: e.matmul(self.PS[3][:, 0:1], self.onesf[:], self.tmpc[0:1, 1:2],
                                      start=True, stop=True),
                   r=["wmean", "onesf", "ps3"], w=["ps3"], tdrain=True)
            sch.Sc(lambda e, col=col: e.copy(self.wsc2[:, col:col + 1], self.PS[3][:, 0:1]),
                   r=["ps3"], w=[f"wsc2{col}"])
            for i, src, wid, loc in tiles(which):
                b = i % 2
                sch.Y(lambda e, s=src, b=b, wid=wid: e.dma_start(wst[b][:, 0:wid], s),
                      r=[f"wst{b}"], w=[f"wst{b}"])
                sch.V(lambda e, b=b, wid=wid, col=col: e.tensor_scalar(
                    wst[b][:, 0:wid], wst[b][:, 0:wid], self.wsc[:, col:col + 1], C_RNE,
                    op0=ALU.mult, op1=ALU.add), r=[f"wsc{col}", f"wst{b}"], w=[f"wst{b}"])
                sch.V(lambda e, b=b, wid=wid: e.tensor_scalar(
                    wst[b][:, 0:wid], wst[b][:, 0:wid], C_RNE, None, op0=ALU.subtract),
                    r=[f"wst{b}"], w=[f"wst{b}"])
                sch.V(lambda e, b=b, wid=wid: e.tensor_scalar(
                    wst[b][:, 0:wid], wst[b][:, 0:wid], 1.0, -1.0, op0=ALU.min, op1=ALU.max),
                    r=[f"wst{b}"], w=[f"wst{b}"])
                sch.V(lambda e, b=b, wid=wid: e.tensor_copy(wqb[:, 0:wid], wst[b][:, 0:wid]),
                      r=[f"wst{b}", "wqb"], w=["wqb"])
                if which == "w1":
                    dc, half = loc
                    dv = ds["w1q_d"][half * 16:(half + 1) * 16, dc, :, :]
                    sch.Y(lambda e, dv=dv: e.dma_start(
                        dv.rearrange("a p c -> p a c"),
                        wqb[:, 0:2048].rearrange("p (a c) -> p a c", a=16)),
                        r=["wqb"], w=["w1qd"])
                else:
                    sch.Y(lambda e, i=loc: e.dma_start(ds["w2q_d"][i, :, :], wqb[:, 0:1024]),
                          r=["wqb"], w=["w2qd"])

        vstg = T_("vstg", [128, 1024], BF16)
        for e_i in range(NE):
            for q in range(4):
                src = di["values"][e_i, q * 128:(q + 1) * 128, :]
                dst = ds["val_d"][e_i, q * 128:(q + 1) * 128, :]
                sch.G(lambda e, s=src: e.dma_start(vstg[:], s), r=["vstg"], w=["vstg"])
                sch.G(lambda e, d=dst: e.dma_start(d, vstg[:]), r=["vstg"], w=["vald"])

    def emit_ln(self, layer):
        sch, ds = self.sch, self.ds
        for tc in range(NT):
            xap = self.x_sb[:, tc * D:(tc + 1) * D]
            sch.V(lambda e, xap=xap, tc=tc: e.reduce_sum(
                self.ln_m[:, tc:tc + 1], xap, axis=AX.X), r=[self.XT[tc]], w=[f"lnm{tc}"])
            sch.Sc(lambda e, xap=xap: e.activation(self.sqt[:], xap, AF.Square),
                   r=[self.XT[tc], "sq"], w=["sq"])
            sch.V(lambda e, tc=tc: e.reduce_sum(self.ln_v[:, tc:tc + 1], self.sqt[:], axis=AX.X),
                  r=["sq"], w=[f"lnv{tc}"])
        rl = [f"lnm{t}" for t in range(NT)] + [f"lnv{t}" for t in range(NT)]
        sch.V(lambda e: e.tensor_scalar(self.ln_m[:], self.ln_m[:], 1.0 / D, None, op0=ALU.mult),
              r=rl, w=["lnM"])
        sch.V(lambda e: e.tensor_scalar(self.ln_v[:], self.ln_v[:], 1.0 / D, None, op0=ALU.mult),
              r=rl, w=["lnV"])
        sch.V(lambda e: e.tensor_mul(self.ln_r[:], self.ln_m[:], self.ln_m[:]),
              r=["lnM", "lnR"], w=["lnR"])
        sch.V(lambda e: e.tensor_sub(self.ln_v[:], self.ln_v[:], self.ln_r[:]),
              r=["lnV", "lnR"], w=["lnV"])
        sch.V(lambda e: e.tensor_scalar(self.ln_v[:], self.ln_v[:], 1e-5, None, op0=ALU.add),
              r=["lnV"], w=["lnV"])
        sch.Sc(lambda e: e.activation(self.ln_r[:], self.ln_v[:], AF.Ln),
               r=["lnV", "lnR"], w=["lnR"])
        sch.Sc(lambda e: e.activation(self.ln_r[:], self.ln_r[:], AF.Exp, scale=-0.5),
               r=["lnR"], w=["lnR"])
        sch.V(lambda e: e.tensor_mul(self.ln_m[:], self.ln_m[:], self.ln_r[:]),
              r=["lnM", "lnR"], w=["lnM"])
        sch.V(lambda e: e.tensor_scalar(self.ln_m[:], self.ln_m[:], -1.0, None, op0=ALU.mult),
              r=["lnM"], w=["lnM"])
        for tc in range(NT):
            b = tc % 2
            xap = self.x_sb[:, tc * D:(tc + 1) * D]
            sch.Sc(lambda e, xap=xap, tc=tc, b=b: e.activation(
                self.xn_t[b][:], xap, AF.Identity, scale=self.ln_r[:, tc:tc + 1],
                bias=self.ln_m[:, tc:tc + 1]),
                r=[self.XT[tc], "lnR", "lnM", f"xn{b}"], w=[f"xn{b}"])
            sch.V(lambda e, b=b: e.tensor_copy(self.xnb[:], self.xn_t[b][:]),
                  r=[f"xn{b}", "xnb"], w=["xnb"])
            sch.Y(lambda e, tc=tc: e.dma_start(
                ds["xn_d"][tc * 128:(tc + 1) * 128, :], self.xnb[:]), r=["xnb"], w=[f"xnd{tc}"])
            if layer == 0:
                self._emit_xq_tile(tc, b)
        for dc in range(ND):
            src = ds["xn_d"][:, dc * 128:(dc + 1) * 128]
            sch.Y(lambda e, src=src, dc=dc: e.dma_start(
                self.xnT[:, dc * S:(dc + 1) * S], src, transpose=True),
                r=[f"xnd{t}" for t in range(NT)] + [f"xnT{dc}"], w=[f"xnT{dc}"])

    def _emit_xq_tile(self, tc, b):
        sch, ds = self.sch, self.ds
        q = self.qsc
        sch.V(lambda e, b=b: e.reduce_max(q[:, 0:1], self.xn_t[b][:], axis=AX.X,
                                          apply_absolute_value=True), r=[f"xn{b}"], w=["qm"])
        sch.V(lambda e: e.tensor_scalar(q[:, 0:1], q[:, 0:1], 1e-5, None, op0=ALU.max),
              r=["qm"], w=["qm"])
        sch.V(lambda e, tc=tc: e.tensor_copy(self.dbg[:, tc:tc + 1], q[:, 0:1]), r=["qm"], w=["dbg"])
        sch.V(lambda e: e.reciprocal(q[:, 1:2], q[:, 0:1]), r=["qm"], w=["qs"])
        sch.V(lambda e, tc=tc: e.tensor_copy(self.dbg[:, 32 + tc:33 + tc], q[:, 1:2]), r=["qs"], w=["dbg"])
        sch.V(lambda e: e.tensor_scalar(q[:, 1:2], q[:, 1:2], 127.0, None, op0=ALU.mult),
              r=["qs"], w=["qs"])
        sch.V(lambda e, tc=tc: e.tensor_copy(self.dbg[:, 48 + tc:49 + tc], q[:, 1:2]), r=["qs"], w=["dbg"])
        sch.V(lambda e: e.tensor_copy(self.qb[:, 0:1], q[:, 1:2]), r=["qs", "qsb"], w=["qsb"])
        sch.V(lambda e: e.tensor_copy(q[:, 1:2], self.qb[:, 0:1]), r=["qsb"], w=["qs2"])
        sch.V(lambda e, tc=tc: e.tensor_copy(self.dbg[:, 16 + tc:17 + tc], q[:, 1:2]), r=["qs2"], w=["dbg"])
        sch.V(lambda e, b=b: e.tensor_scalar(self.xn_t[b][:], self.xn_t[b][:], q[:, 1:2], C_RNE,
                                             op0=ALU.mult, op1=ALU.add),
              r=["qs2", f"xn{b}"], w=[f"xn{b}"])
        sch.V(lambda e, b=b: e.tensor_scalar(self.xn_t[b][:], self.xn_t[b][:], C_RNE, None,
                                             op0=ALU.subtract), r=[f"xn{b}"], w=[f"xn{b}"])
        sch.V(lambda e, b=b: e.tensor_copy(self.xnb[:], self.xn_t[b][:]),
              r=[f"xn{b}", "xnb"], w=["xnb"])
        sch.Y(lambda e, tc=tc: e.dma_start(
            ds["xq_d"][tc * 128:(tc + 1) * 128, :], self.xnb[:]), r=["xnb"], w=[f"xqd{tc}"])
        sch.V(lambda e, tc=tc: e.reciprocal(self.rq[:, tc:tc + 1], q[:, 1:2]),
              r=["qs2"], w=[f"rq{tc}"])

    def alloc_attn(self, T_, sfx=""):
        self.Qp = T_("Qp" + sfx, [128, S], BF16)
        self.Kp = T_("Kp" + sfx, [128, S], BF16)
        self.Vtok = T_("Vtok" + sfx, [128, NT * 128], BF16)
        self.Qh = [T_(f"Qh{i}" + sfx, [66, S], BF16) for i in range(2)]
        self.Kh = [T_(f"Kh{i}" + sfx, [66, S], BF16) for i in range(2)]
        self.Pt = [T_(f"Pt{i}" + sfx, [128, S], BF16) for i in range(2)]
        self.wq_sb = T_("wq_sb" + sfx, [128, ND * 128], BF16)
        self.wk_sb = T_("wk_sb" + sfx, [128, ND * 128], BF16)
        self.wv_sb = T_("wv_sb" + sfx, [128, ND * 128], BF16)
        self.wo_all = T_("wo_all" + sfx, [128, 8 * 1024], BF16)
        self.oT = T_("oT" + sfx, [128, S], BF16)

    def _cast_w_cols(self, dst_sb, dst_res, wT, c0, w):
        for dc in range(ND):
            src = wT[dc * 128:(dc + 1) * 128, c0:c0 + w]
            self.sch.G(lambda e, src=src, dc=dc, w=w: e.dma_start(
                dst_sb[:, dc * 128:dc * 128 + w], src), r=[dst_res], w=[dst_res])

    def emit_attn(self, layer):
        sch, di, ds = self.sch, self.di, self.ds
        PS, PR = self.PS, self.PR
        xnTr = [f"xnT{dc}" for dc in range(ND)]
        scale = 1.0 if layer == 0 else L1_SCALE

        if layer == 0:
            self._cast_w_cols(self.wk_sb, "wk_sb", di["wkT0"], 0, DK)
            self._cast_w_cols(self.wv_sb, "wv_sb", di["wvT0"], 0, DK)
            for nq in range(4):
                for dc in range(ND):
                    sch.Tn(lambda e, nq=nq, dc=dc: e.matmul(
                        PS[0][0:DK, 0:512],
                        self.wk_sb[:, dc * 128:dc * 128 + DK],
                        self.xnT[:, dc * S + nq * 512:dc * S + nq * 512 + 512],
                        start=(dc == 0), stop=(dc == ND - 1)),
                        r=["wk_sb", "ps0"] + xnTr, w=["ps0"])
                for half in (0, 1):
                    sch.Sc(lambda e, nq=nq, half=half: e.copy(
                        self.Kp[half * 64:half * 64 + 64, nq * 512:(nq + 1) * 512],
                        PS[0][0:DK, 0:512]), r=["ps0", "Kp"], w=["Kp"])
            for tc in range(NT):
                for dc in range(ND):
                    sch.Tn(lambda e, tc=tc, dc=dc: e.matmul(
                        PS[1][:, 0:DK],
                        self.xnT[:, dc * S + tc * 128:dc * S + tc * 128 + 128],
                        self.wv_sb[:, dc * 128:dc * 128 + DK],
                        start=(dc == 0), stop=(dc == ND - 1)),
                        r=["wv_sb", "ps1"] + xnTr, w=["ps1"])
                sch.Sc(lambda e, tc=tc: e.copy(self.Vtok[:, tc * DK:(tc + 1) * DK],
                                               PS[1][:, 0:DK]), r=["ps1", "Vtok"], w=["Vtok"])
            sch.V(lambda e: e.tensor_copy(self.Kh[0][0:64, :], self.Kp[0:64, :]),
                  r=["Kp", "Kh0"], w=["Kh0"])
            sch.V(lambda e: e.memset(self.Kh[0][64:66, :], 1.0), w=["Kh0"])

        wqT = di["wqT0"] if layer == 0 else di["wqT1"]
        for p in range(8):
            self._attn_pair(layer, p, wqT, scale)
        self._outproj(layer)

    def _attn_pair(self, layer, p, wqT, scale):
        sch, di, ds = self.sch, self.di, self.ds
        PS, PR = self.PS, self.PR
        xnTr = [f"xnT{dc}" for dc in range(ND)]

        self._cast_w_cols(self.wq_sb, "wq_sb", wqT, p * 128, 128)
        for nq in range(4):
            ps = PS[0] if nq < 2 else PS[1]
            pr = "ps0" if nq < 2 else "ps1"
            off = (nq % 2) * 512
            for dc in range(ND):
                sch.Tn(lambda e, nq=nq, dc=dc, ps=ps, off=off: e.matmul(
                    ps[:, off:off + 512],
                    self.wq_sb[:, dc * 128:(dc + 1) * 128],
                    self.xnT[:, dc * S + nq * 512:dc * S + nq * 512 + 512],
                    start=(dc == 0), stop=(dc == ND - 1)),
                    r=["wq_sb", pr] + xnTr, w=[pr])
        for nq in range(4):
            ps = PS[0] if nq < 2 else PS[1]
            pr = "ps0" if nq < 2 else "ps1"
            cap = ps[:, (nq % 2) * 512:(nq % 2) * 512 + 512]
            if layer == 0:
                sch.V(lambda e, cap=cap, nq=nq: e.tensor_scalar(
                    self.Qp[:, nq * 512:(nq + 1) * 512], cap, self.smsc[:, 0:1], None,
                    op0=ALU.mult), r=[pr, "smsc", "Qp"], w=["Qp"])
            else:
                sch.Sc(lambda e, cap=cap, nq=nq: e.copy(
                    self.Qp[:, nq * 512:(nq + 1) * 512], cap), r=[pr, "Qp"], w=["Qp"])
        if layer == 1:
            self._cast_w_cols(self.wk_sb, "wk_sb", di["wkT1"], p * 128, 128)
            self._cast_w_cols(self.wv_sb, "wv_sb", di["wvT1"], p * 128, 128)
            for nq in range(4):
                off = (nq % 2) * 512
                for dc in range(ND):
                    sch.Tn(lambda e, nq=nq, dc=dc, off=off: e.matmul(
                        PS[2][:, off:off + 512],
                        self.wk_sb[:, dc * 128:(dc + 1) * 128],
                        self.xnT[:, dc * S + nq * 512:dc * S + nq * 512 + 512],
                        start=(dc == 0), stop=(dc == ND - 1)),
                        r=["wk_sb", "ps2"] + xnTr, w=["ps2"])
                sch.Sc(lambda e, nq=nq, off=off: e.copy(
                    self.Kp[:, nq * 512:(nq + 1) * 512], PS[2][:, off:off + 512]),
                    r=["ps2", "Kp"], w=["Kp"])
            for tc in range(NT):
                for dc in range(ND):
                    sch.Tn(lambda e, tc=tc, dc=dc: e.matmul(
                        PS[3][:, 0:128],
                        self.xnT[:, dc * S + tc * 128:dc * S + tc * 128 + 128],
                        self.wv_sb[:, dc * 128:(dc + 1) * 128],
                        start=(dc == 0), stop=(dc == ND - 1)),
                        r=["wv_sb", "ps3"] + xnTr, w=["ps3"])
                sch.Sc(lambda e, tc=tc: e.copy(self.Vtok[:, tc * 128:(tc + 1) * 128],
                                               PS[3][:, 0:128]), r=["ps3", "Vtok"], w=["Vtok"])

        for sc in range(NT):
            twid = S if layer == 0 else 128 * (sc + 1)
            for half_id in (0, 1):
                lo = half_id * 1024
                hi = min(twid, lo + 1024)
                if lo >= hi:
                    continue
                ps_e = PS[0] if half_id == 0 else PS[1]
                ps_o = PS[2] if half_id == 0 else PS[3]
                pr_e = PR[0] if half_id == 0 else PR[1]
                pr_o = PR[2] if half_id == 0 else PR[3]
                for (c0, c1) in _chunks(lo, hi):
                    sch.Tn(lambda e, c0=c0, c1=c1, sc=sc, ps=ps_e, lo=lo: e.matmul(
                        ps[:, c0 - lo:c1 - lo],
                        self.Qp[0:64, sc * 128:(sc + 1) * 128],
                        self.Kp[0:64, c0:c1], start=True, stop=True, tile_position=(0, 0)),
                        r=["Qp", "Kp", pr_e], w=[pr_e])
                    sch.Tn(lambda e, c0=c0, c1=c1, sc=sc, ps=ps_o, lo=lo: e.matmul(
                        ps[:, c0 - lo:c1 - lo],
                        self.Qp[64:128, sc * 128:(sc + 1) * 128],
                        self.Kp[64:128, c0:c1], start=True, stop=True, tile_position=(64, 0)),
                        r=["Qp", "Kp", pr_o], w=[pr_o])
            if layer == 1:
                dlo = sc * 128
                h_id = 0 if dlo < 1024 else 1
                off = dlo % 1024
                pse = (PS[0], PR[0]) if h_id == 0 else (PS[1], PR[1])
                pso = (PS[2], PR[2]) if h_id == 0 else (PS[3], PR[3])
                for ps, pr in (pse, pso):
                    sch.V(lambda e, ps=ps, off=off: e.tensor_add(
                        ps[:, off:off + 128], ps[:, off:off + 128], self.mst[:]),
                        r=[pr, "mst"], w=[pr])
            for hh in range(2):
                psA = PS[0] if hh == 0 else PS[2]
                psB = PS[1] if hh == 0 else PS[3]
                prA = PR[0] if hh == 0 else PR[2]
                prB = PR[1] if hh == 0 else PR[3]
                col = hh * NT + sc
                wA = min(twid, 1024)
                wB = twid - wA
                sch.V(lambda e, psA=psA, wA=wA, col=col: e.reduce_max(
                    self.nmt[:, col:col + 1], psA[:, 0:wA], axis=AX.X, negate=True),
                    r=[prA], w=[f"nm{col}"])
                if wB > 0:
                    sch.V(lambda e, psB=psB, wB=wB: e.reduce_max(
                        self.tmpc[:, 0:1], psB[:, 0:wB], axis=AX.X, negate=True),
                        r=[prB], w=["nmB"])
                    sch.V(lambda e, col=col: e.tensor_tensor(
                        self.nmt[:, col:col + 1], self.nmt[:, col:col + 1],
                        self.tmpc[:, 0:1], op=ALU.min), r=["nmB", f"nm{col}"], w=[f"nm{col}"])
                if layer == 1:
                    sch.V(lambda e, col=col: e.tensor_scalar(
                        self.tmpc[:, 1:2], self.nmt[:, col:col + 1], L1_SCALE, None,
                        op0=ALU.mult), r=[f"nm{col}"], w=["biasc"])
                    biasap, biasres = self.tmpc[:, 1:2], "biasc"
                else:
                    biasap, biasres = self.nmt[:, col:col + 1], f"nm{col}"
                sch.Sc(lambda e, psA=psA, wA=wA, col=col, biasap=biasap: e.activation(
                    self.junk[:, 0:wA], psA[:, 0:wA], AF.Exp, bias=biasap, scale=scale,
                    accum_out=self.rt[:, col:col + 1]),
                    r=[prA, biasres, "junk"], w=["junk", f"rr{col}"])
                if wB > 0:
                    sch.Sc(lambda e, psB=psB, wB=wB, biasap=biasap: e.activation(
                        self.junk[:, 0:wB], psB[:, 0:wB], AF.Exp, bias=biasap, scale=scale,
                        accum_out=self.tmpc[:, 2:3]),
                        r=[prB, biasres, "junk"], w=["junk", "rrB"])
                    sch.V(lambda e, col=col: e.tensor_add(
                        self.rt[:, col:col + 1], self.rt[:, col:col + 1], self.tmpc[:, 2:3]),
                        r=["rrB", f"rr{col}"], w=[f"rr{col}"])

        for hh in range(2):
            cols = slice(hh * NT, hh * NT + NT)
            rres = [f"rr{hh * NT + s}" for s in range(NT)]
            nres = [f"nm{hh * NT + s}" for s in range(NT)]
            sch.Sc(lambda e, cols=cols: e.activation(self.lr_t[:], self.rt[:, cols], AF.Ln),
                   r=rres + ["lnr"], w=["lnr"])
            if layer == 1:
                sch.V(lambda e: e.tensor_scalar(self.lr_t[:], self.lr_t[:], 1.0 / L1_SCALE,
                                                None, op0=ALU.mult), r=["lnr"], w=["lnr"])
            sch.V(lambda e, cols=cols: e.tensor_sub(self.augf[:], self.nmt[:, cols],
                                                    self.lr_t[:]),
                  r=nres + ["lnr", "augf"], w=["augf"])
            sch.V(lambda e: e.tensor_copy(self.augb[:, 0:NT], self.augf[:]),
                  r=["augf", "augb"], w=["augb"])
            sch.V(lambda e: e.tensor_copy(self.aug2[:], self.augb[:, 0:NT]),
                  r=["augb", "augu"], w=["augu"])
            sch.V(lambda e: e.tensor_sub(self.aug2[:], self.augf[:], self.aug2[:]),
                  r=["augf", "augu"], w=["augu"])
            sch.V(lambda e: e.tensor_copy(self.augb[:, NT:2 * NT], self.aug2[:]),
                  r=["augu", "augb"], w=["augb"])
            self.col2row(self.augb[:, 0:32], 32, 0, ["augb"], "rowd")
            sch.V(lambda e, hh=hh: e.tensor_copy(self.Qh[hh][0:64, :],
                                                 self.Qp[hh * 64:hh * 64 + 64, :]),
                  r=["Qp", f"Qh{hh}"], w=[f"Qh{hh}"])
            sch.Y(lambda e, hh=hh: e.dma_start(
                self.Qh[hh][64:66, 0:S],
                ds["row_d"][0:32, 0:128].rearrange("(r a) c -> r (a c)", r=2)),
                r=["rowd", f"Qh{hh}"], w=[f"Qh{hh}"])
            if layer == 1:
                sch.V(lambda e, hh=hh: e.tensor_copy(self.Kh[hh][0:64, :],
                                                     self.Kp[hh * 64:hh * 64 + 64, :]),
                      r=["Kp", f"Kh{hh}"], w=[f"Kh{hh}"])
                sch.V(lambda e, hh=hh: e.memset(self.Kh[hh][64:66, :], 1.0), w=[f"Kh{hh}"])

        for tc in range(NT):
            slo = 0 if layer == 0 else tc * 128
            for hh in range(2):
                kh = self.Kh[0] if layer == 0 else self.Kh[hh]
                khres = "Kh0" if layer == 0 else f"Kh{hh}"
                psx, prx = PS[hh], PR[hh]
                for half in (0, 1):
                    c0h = max(slo, half * 1024)
                    c1h = (half + 1) * 1024
                    if c0h >= c1h:
                        continue
                    for (c0, c1) in _chunks(c0h, c1h):
                        sch.Tn(lambda e, c0=c0, c1=c1, tc=tc, kh=kh, hh=hh, half=half,
                               psx=psx: e.matmul(
                            psx[:, c0 - half * 1024:c1 - half * 1024],
                            kh[0:66, tc * 128:(tc + 1) * 128],
                            self.Qh[hh][0:66, c0:c1], start=True, stop=True),
                            r=[khres, f"Qh{hh}", prx], w=[prx])
                    if layer == 1 and half * 1024 <= tc * 128 < (half + 1) * 1024:
                        off = tc * 128 - half * 1024
                        sch.V(lambda e, psx=psx, off=off: e.tensor_add(
                            psx[:, off:off + 128], psx[:, off:off + 128], self.mts[:]),
                            r=[prx, "mts"], w=[prx])
                    w0 = c0h - half * 1024
                    sch.Sc(lambda e, psx=psx, w0=w0, hh=hh, c0h=c0h, c1h=c1h: e.activation(
                        self.Pt[hh][:, c0h:c1h], psx[:, w0:1024], AF.Exp, scale=scale),
                        r=[prx, f"Pt{hh}"], w=[f"Pt{hh}"])
            for half in (0, 1):
                psz = PS[3] if half == 0 else PS[2]
                prz = PR[3] if half == 0 else PR[2]
                c0h = max(slo, half * 1024)
                c1h = (half + 1) * 1024
                if c0h >= c1h:
                    continue
                first = (tc == 0)
                last = (tc == NT - 1) if layer == 0 else (tc == min(NT - 1, (c1h - 1) // 128))
                for hh, tp in ((0, (0, 0)), (1, (0, 64))):
                    if layer == 0:
                        vap = self.Vtok[:, tc * DK:(tc + 1) * DK]
                    else:
                        vap = self.Vtok[:, tc * 128 + hh * DK:tc * 128 + hh * DK + DK]
                    for (c0, c1) in _chunks(c0h, c1h):
                        sch.Tn(lambda e, c0=c0, c1=c1, hh=hh, tp=tp, vap=vap, psz=psz,
                               half=half, first=first, last=last: e.matmul(
                            psz[hh * 64:hh * 64 + 64, c0 - half * 1024:c1 - half * 1024],
                            vap, self.Pt[hh][:, c0:c1],
                            start=first, stop=last, tile_position=tp),
                            r=["Vtok", f"Pt{hh}", prz], w=[prz])
        for half in (0, 1):
            psz = PS[3] if half == 0 else PS[2]
            prz = PR[3] if half == 0 else PR[2]
            sch.Sc(lambda e, psz=psz, half=half: e.copy(
                self.oT[:, half * 1024:(half + 1) * 1024], psz[:, 0:1024]),
                r=[prz, "oT"], w=["oT"])
        sch.Y(lambda e, p=p: e.dma_start(self.ds["oT_d"][p, :, :], self.oT[:]),
              r=["oT"], w=[f"oTd{p}"])

    def _outproj(self, layer):
        sch, di, ds = self.sch, self.di, self.ds
        PS, PR = self.PS, self.PR
        woT = di["woT0"] if layer == 0 else di["wpT1"]
        for p in range(8):
            src = woT[p * 128:(p + 1) * 128, :]
            sch.G(lambda e, src=src, p=p: e.dma_start(
                self.wo_all[:, p * 1024:(p + 1) * 1024], src), r=["wo_all"], w=["wo_all"])
        cmul = 1.0 if layer == 0 else L1_OUT_SCALE
        for tc in range(NT):
            b = tc % 2
            ps, pr = PS[b], PR[b]
            for p in range(8):
                sch.Y(lambda e, p=p, tc=tc: e.dma_start(
                    self.oT[:, ((p * NT + tc) % 16) * 128:((p * NT + tc) % 16) * 128 + 128],
                    ds["oT_d"][p, :, tc * 128:(tc + 1) * 128]),
                    r=[f"oTd{p}", "oT"], w=["oT"])
                for (c0, c1) in ((0, 512), (512, 1024)):
                    sch.Tn(lambda e, p=p, tc=tc, c0=c0, c1=c1, ps=ps: e.matmul(
                        ps[:, c0:c1],
                        self.oT[:, ((p * NT + tc) % 16) * 128:((p * NT + tc) % 16) * 128 + 128],
                        self.wo_all[:, p * 1024 + c0:p * 1024 + c1],
                        start=(p == 0), stop=(p == 7)),
                        r=["oT", "wo_all", pr], w=[pr])
            sch.V(lambda e, tc=tc, ps=ps, pr=pr, cmul=cmul: e.scalar_tensor_tensor(
                self.x_sb[:, tc * D:(tc + 1) * D], ps[:, 0:1024], cmul,
                self.x_sb[:, tc * D:(tc + 1) * D], op0=ALU.mult, op1=ALU.add),
                r=[pr, self.XT[tc]], w=[self.XT[tc]])

    def emit_ffn(self, T_):
        sch, ds = self.sch, self.ds
        PS, PR = self.PS, self.PR
        xqT = T_("xqT", [128, ND * 768], BF16)
        w1t = [T_(f"w1t{i}", [128, ND * 128], BF16) for i in range(2)]
        w2t = [T_(f"w2t{i}", [128, 1024], BF16) for i in range(2)]
        hs = T_("hs", [128, 32 * 512])
        hsr = T_("hsr", [128, 516])
        hqt = [T_(f"hqt{i}", [128, 512], BF16) for i in range(2)]
        habs = T_("habs", [128, 512])
        sbc = T_("sbc", [128, 516])
        sbq = T_("sbq", [128, 512])
        rxp = T_("rxp", [1, 2052], BF16)
        sqrow = T_("sqrow", [1, 512], BF16)
        rh = T_("rh", [128, 4])
        hcol = T_("hcol", [128, 4])
        rqm = T_("rqm", [128, NT])
        rqb = T_("rqb", [128, NT], BF16)

        sch.V(lambda e: e.tensor_scalar(rqm[:], self.rq[:], self.wsc2[:, 0:1], None,
                                        op0=ALU.mult),
              r=[f"rq{t}" for t in range(NT)] + ["wsc20"], w=["rqm"])
        sch.V(lambda e: e.tensor_copy(rqb[:], rqm[:]), r=["rqm", "rqb"], w=["rqb"])
        self.col2row(rqb[:, 0:NT], NT, 32, ["rqb"], "rxrow_d")
        sch.V(lambda e: e.memset(rxp[:], 0.0), r=["rxp"], w=["rxp"])
        sch.Y(lambda e: e.dma_start(
            rxp[0:1, 1:2049], ds["row_d"][32:48, 0:128].rearrange("(r a) c -> r (a c)", r=1)),
            r=["rxrow_d", "rxp"], w=["rxp"])

        conv_r = ["convs"]
        for ck in range(4):
            t0 = ck * 512
            ckr = f"ck{ck}"
            for dc in range(ND):
                lo_blk = 0 if ck == 0 else -1
                hi_blk = 5 if ck == 3 else 6
                srow0 = t0 - 128 + (128 if ck == 0 else 0)
                nrows = 640 if (ck == 0 or ck == 3) else 768
                src = ds["xq_d"][srow0:srow0 + nrows, dc * 128:(dc + 1) * 128]
                doff = dc * 768 + (128 if ck == 0 else 0)
                sch.Y(lambda e, src=src, doff=doff, nrows=nrows: e.dma_start(
                    self.xqT_view(xqT, doff, nrows), src, transpose=True),
                    r=[f"xqd{t}" for t in range(NT)] + ["xqT"], w=["xqT"])
                if ck == 0:
                    sch.V(lambda e, dc=dc: e.memset(xqT[:, dc * 768:dc * 768 + 128], 0.0),
                          r=["xqT"], w=["xqT"])
                if ck == 3:
                    sch.V(lambda e, dc=dc: e.memset(xqT[:, dc * 768 + 640:dc * 768 + 768], 0.0),
                          r=["xqT"], w=["xqT"])
            for (c0, c1) in _chunks(0, 514):
                sch.Tn(lambda e, c0=c0, c1=c1, t0=t0: e.matmul(
                    self.PS[2][:, c0:c1], self.onesb[:], rxp[0:1, t0 + c0:t0 + c1],
                    start=True, stop=True), r=["rxp", "onesb", "ps2"], w=["ps2"], tdrain=True)
            sch.Sc(lambda e: e.copy(sbc[:, 0:514], self.PS[2][:, 0:514]),
                   r=["ps2", "sbc"], w=["sbc"])
            for hc in range(32):
                wb = hc % 2
                src = ds["w1q_d"][hc, :, :, :]
                sch.Y(lambda e, src=src, wb=wb: e.dma_start(
                    w1t[wb][:].rearrange("p (a c) -> p a c", a=ND),
                    src.rearrange("a p c -> p a c")),
                    r=[f"w1t{wb}"], w=[f"w1t{wb}"])
                pb = PS[hc % 2]
                pbr = PR[hc % 2]
                for dc in range(ND):
                    xoff = dc * 768 + 127
                    sch.Tn(lambda e, dc=dc, wb=wb, pb=pb, xoff=xoff: e.matmul(
                        pb[:, 0:512],
                        w1t[wb][:, dc * 128:(dc + 1) * 128],
                        xqT[:, xoff:xoff + 512],
                        start=(dc == 0), stop=(dc == ND - 1)),
                        r=[f"w1t{wb}", "xqT", pbr], w=[pbr])
                    sch.Tn(lambda e, dc=dc, wb=wb, pb=pb, xoff=xoff: e.matmul(
                        pb[:, 512:514],
                        w1t[wb][:, dc * 128:(dc + 1) * 128],
                        xqT[:, xoff + 512:xoff + 514],
                        start=(dc == 0), stop=(dc == ND - 1)),
                        r=[f"w1t{wb}", "xqT", pbr], w=[pbr])
                sch.V(lambda e, pb=pb, pbr=pbr: e.tensor_mul(hsr[:, 0:514], pb[:, 0:514],
                                                             sbc[:, 0:514]),
                      r=[pbr, "sbc", "hsr"], w=["hsr"])
                hslice = hs[:, hc * 512:(hc + 1) * 512]
                hres = f"hs{hc}"
                sch.Sc(lambda e, hslice=hslice, hc=hc: e.activation(
                    hslice, hsr[:, 1:513], AF.Identity,
                    scale=self.convs[:, hc * 4 + 1:hc * 4 + 2],
                    bias=self.convs[:, hc * 4 + 3:hc * 4 + 4]),
                    r=["hsr", "convs", hres], w=[hres])
                sch.V(lambda e, hslice=hslice, hc=hc: e.scalar_tensor_tensor(
                    hslice, hsr[:, 0:512], self.convs[:, hc * 4 + 0:hc * 4 + 1], hslice,
                    op0=ALU.mult, op1=ALU.add), r=["hsr", "convs", hres], w=[hres])
                sch.V(lambda e, hslice=hslice, hc=hc: e.scalar_tensor_tensor(
                    hslice, hsr[:, 2:514], self.convs[:, hc * 4 + 2:hc * 4 + 3], hslice,
                    op0=ALU.mult, op1=ALU.add), r=["hsr", "convs", hres], w=[hres])
                sch.Sc(lambda e, hslice=hslice: e.activation(hslice, hslice, AF.Silu),
                       r=[hres], w=[hres])
                if hc == 0:
                    sch.V(lambda e: e.memset(habs[:], 0.0), r=["habs"], w=["habs"])
                sch.Sc(lambda e, hslice=hslice: e.activation(self.sqt[:, 0:512], hslice, AF.Abs),
                       r=[hres, "sq"], w=["sq"])
                sch.V(lambda e: e.tensor_tensor(habs[:], habs[:], self.sqt[:, 0:512],
                                                op=ALU.max), r=["sq", "habs"], w=["habs"])
            for tcl in range(4):
                sch.Tn(lambda e, tcl=tcl: e.transpose(
                    self.PS[2][:, tcl * 128:(tcl + 1) * 128],
                    habs[:, tcl * 128:(tcl + 1) * 128], self.identf[:]),
                    r=["habs", "identf", "ps2"], w=["ps2"], tdrain=True)
                sch.V(lambda e, tcl=tcl: e.reduce_max(
                    hcol[:, tcl:tcl + 1], self.PS[2][:, tcl * 128:(tcl + 1) * 128], axis=AX.X),
                    r=["ps2"], w=["hm"])
                sch.V(lambda e, tcl=tcl: e.tensor_scalar(
                    hcol[:, tcl:tcl + 1], hcol[:, tcl:tcl + 1], 1e-5, None, op0=ALU.max),
                    r=["hm"], w=["hm"])
            sch.V(lambda e: e.reciprocal(rh[:, 0:4], hcol[:, 0:4]), r=["hm"], w=["rh0"])
            sch.V(lambda e: e.tensor_scalar(rh[:, 0:4], rh[:, 0:4], 127.0, None, op0=ALU.mult),
                  r=["rh0"], w=["rh0"])
            sch.V(lambda e: e.tensor_copy(self.qb[:, 0:4], rh[:, 0:4]),
                  r=["rh0", "qsb"], w=["qsb"])
            sch.V(lambda e: e.tensor_copy(rh[:, 0:4], self.qb[:, 0:4]),
                  r=["qsb"], w=["sh2"])
            self.col2row(self.qb[:, 0:4], 4, 48, ["qsb"], "shrow_d")
            sch.Y(lambda e: e.dma_start(
                sqrow[0:1, :], ds["row_d"][48:52, 0:128].rearrange("(r a) c -> r (a c)", r=1)),
                r=["shrow_d", "sqrow"], w=["sqrow"])
            for (c0, c1) in _chunks(0, 512):
                sch.Tn(lambda e, c0=c0, c1=c1: e.matmul(
                    self.PS[3][:, c0:c1], self.onesb[:], sqrow[0:1, c0:c1],
                    start=True, stop=True), r=["sqrow", "onesb", "ps3"], w=["ps3"], tdrain=True)
            sch.Sc(lambda e: e.copy(sbq[:], self.PS[3][:, 0:512]), r=["ps3", "sbq"], w=["sbq"])
            sch.V(lambda e: e.reciprocal(rh[:, 0:4], rh[:, 0:4]), r=["sh2", "rh0"], w=["rh0"])
            sch.V(lambda e: e.tensor_scalar(rh[:, 0:4], rh[:, 0:4], self.wsc2[:, 1:2], None,
                                            op0=ALU.mult), r=["rh0", "wsc21"], w=["rh0"])
            for hc in range(32):
                hslice = hs[:, hc * 512:(hc + 1) * 512]
                hres = f"hs{hc}"
                hb = hc % 2
                sch.V(lambda e, hslice=hslice: e.tensor_mul(hslice, hslice, sbq[:]),
                      r=[hres, "sbq"], w=[hres])
                sch.V(lambda e, hslice=hslice: e.tensor_scalar(
                    hslice, hslice, C_RNE, C_RNE, op0=ALU.add, op1=ALU.subtract),
                    r=[hres], w=[hres])
                sch.V(lambda e, hslice=hslice, hb=hb: e.tensor_copy(
                    hqt[hb][:], hslice), r=[hres, f"hqt{hb}"], w=[f"hqt{hb}"])
                wb = hc % 2
                sch.Y(lambda e, hc=hc, wb=wb: e.dma_start(w2t[wb][:], ds["w2q_d"][hc, :, :]),
                      r=[f"w2t{wb}"], w=[f"w2t{wb}"])
                for tcl in range(4):
                    for (c0, c1) in ((0, 512), (512, 1024)):
                        sch.Tn(lambda e, hb=hb, wb=wb, tcl=tcl, c0=c0, c1=c1, hc=hc: e.matmul(
                            PS[tcl][:, c0:c1],
                            hqt[hb][:, tcl * 128:(tcl + 1) * 128],
                            w2t[wb][:, c0:c1],
                            start=(hc == 0), stop=(hc == 31)),
                            r=[f"hqt{hb}", f"w2t{wb}", PR[tcl]], w=[PR[tcl]])
            for tcl in range(4):
                tc = ck * 4 + tcl
                sch.V(lambda e, tc=tc, tcl=tcl: e.scalar_tensor_tensor(
                    self.x_sb[:, tc * D:(tc + 1) * D], PS[tcl][:, 0:1024], rh[:, tcl:tcl + 1],
                    self.x_sb[:, tc * D:(tc + 1) * D], op0=ALU.mult, op1=ALU.add),
                    r=[PR[tcl], "rh0", self.XT[tc]], w=[self.XT[tc]])

    @staticmethod
    def xqT_view(xqT, doff, nrows):
        return xqT[:, doff:doff + nrows]

    def emit_moe(self, T_):
        sch, di, ds = self.sch, self.di, self.ds
        PS, PR = self.PS, self.PR
        xnTr = [f"xnT{dc}" for dc in range(ND)]
        selb = T_("selb", [128, ND * 32], BF16)
        lgf = T_("lgf", [32, 2048])
        ltok = T_("ltok", [128, 512])
        gtok = T_("gtok", [128, 512])
        gwork = T_("gwork", [128, 32])
        geq = T_("geq", [128, 32])
        gmax = T_("gmax", [128, 4])
        gfm = T_("gfm", [32, 2048])
        gbf = T_("gbf", [32, 2048], BF16)
        gbc = T_("gbc", [128, 2048])
        kbf = [T_(f"kbf{i}", [128, ND * 512], BF16) for i in range(2)]
        sgs = T_("sgs", [128, 2048], BF16)
        sgt = [T_(f"sgt{i}", [128, 512], BF16) for i in range(2)]
        sel1h = T_("sel1h_sb", [32, NE * 128], BF16)
        sch.Y(lambda e: e.dma_start(sel1h[:], di["sel1h"]), w=["sel1h"])
        vbt = [T_(f"vbt{i}", [128, 1024], BF16) for i in range(2)]

        for dc in range(ND):
            src = di["selT"][dc * 128:(dc + 1) * 128, :]
            sch.G(lambda e, src=src, dc=dc: e.dma_start(selb[:, dc * 32:(dc + 1) * 32], src),
                  r=["selb"], w=["selb"])
        for nq in range(4):
            ps = PS[0] if nq < 2 else PS[1]
            pr = PR[0] if nq < 2 else PR[1]
            off = (nq % 2) * 512
            for dc in range(ND):
                sch.Tn(lambda e, nq=nq, dc=dc, ps=ps, off=off: e.matmul(
                    ps[0:32, off:off + 512],
                    selb[:, dc * 32:(dc + 1) * 32],
                    self.xnT[:, dc * S + nq * 512:dc * S + nq * 512 + 512],
                    start=(dc == 0), stop=(dc == ND - 1)),
                    r=["selb", pr] + xnTr, w=[pr])
            sch.Sc(lambda e, nq=nq, ps=ps, off=off: e.copy(
                lgf[:, nq * 512:(nq + 1) * 512], ps[0:32, off:off + 512]),
                r=[pr, "lgf"], w=["lgf"])
        for tc in range(NT):
            sch.Tn(lambda e, tc=tc: e.transpose(
                self.PS[2][0:128, 0:32], lgf[0:32, tc * 128:(tc + 1) * 128],
                self.identf[0:32, 0:32]), r=["lgf", "identf", "ps2"], w=["ps2"], tdrain=True)
            sch.Sc(lambda e, tc=tc: e.copy(ltok[:, tc * 32:(tc + 1) * 32],
                                           self.PS[2][0:128, 0:32]),
                   r=["ps2", "ltok"], w=["ltok"])
        sch.Sc(lambda e: e.activation(gtok[:], ltok[:], AF.Sigmoid), r=["ltok", "gtok"], w=["gtok"])
        for tc in range(NT):
            gt = gtok[:, tc * 32:(tc + 1) * 32]
            sch.V(lambda e, gt=gt: e.tensor_copy(gwork[:], gt), r=["gtok", "gwork"], w=["gwork"])
            for it in range(4):
                sch.V(lambda e, it=it: e.reduce_max(gmax[:, it:it + 1], gwork[:], axis=AX.X),
                      r=["gwork"], w=["gmax"])
                if it < 3:
                    sch.V(lambda e, it=it: e.tensor_scalar(
                        geq[:], gwork[:], gmax[:, it:it + 1], None, op0=ALU.is_ge),
                        r=["gwork", "gmax"], w=["geq"])
                    sch.V(lambda e: e.scalar_tensor_tensor(
                        gwork[:], geq[:], NEG_BIG, gwork[:], op0=ALU.mult, op1=ALU.add),
                        r=["geq", "gwork"], w=["gwork"])
            sch.V(lambda e, gt=gt: e.tensor_scalar(geq[:], gt, gmax[:, 3:4], None, op0=ALU.is_ge),
                  r=["gtok", "gmax"], w=["geq"])
            sch.V(lambda e, gt=gt, tc=tc: e.tensor_tensor(
                gtok[:, tc * 32:(tc + 1) * 32], gt, geq[:], op=ALU.mult),
                r=["geq", "gtok"], w=["gtok"])
            sch.Tn(lambda e, tc=tc: e.transpose(
                self.PS[2][0:32, 0:128], gtok[:, tc * 32:(tc + 1) * 32], self.identf[:]),
                r=["gtok", "identf", "ps2"], w=["ps2"], tdrain=True)
            sch.Sc(lambda e, tc=tc: e.copy(gfm[0:32, tc * 128:(tc + 1) * 128],
                                           self.PS[2][0:32, 0:128]),
                   r=["ps2", "gfm"], w=["gfm"])
        sch.V(lambda e: e.tensor_copy(gbf[:], gfm[0:32, :]), r=["gfm", "gbf"], w=["gbf"])

        for e_i in range(NE):
            kb = e_i % 2
            for dc in range(ND):
                src = di["keys"][e_i, dc * 128:(dc + 1) * 128, :]
                sch.G(lambda e, src=src, dc=dc, kb=kb: e.dma_start(
                    kbf[kb][:, dc * 512:(dc + 1) * 512], src), r=[f"kbf{kb}"], w=[f"kbf{kb}"])
            for (c0, c1) in _chunks(0, 2048):
                pz = self.PS[2] if c0 < 1024 else self.PS[3]
                pzr = PR[2] if c0 < 1024 else PR[3]
                sch.Tn(lambda e, c0=c0, c1=c1, pz=pz, e_i=e_i: e.matmul(
                    pz[:, c0 % 1024:c0 % 1024 + (c1 - c0)],
                    sel1h[0:32, e_i * 128:(e_i + 1) * 128],
                    gbf[0:32, c0:c1], start=True, stop=True),
                    r=["gbf", "sel1h", pzr], w=[pzr], tdrain=True)
            sch.Sc(lambda e: e.copy(gbc[:, 0:1024], self.PS[2][:, 0:1024]),
                   r=[PR[2], "gbc"], w=["gbc"])
            sch.Sc(lambda e: e.copy(gbc[:, 1024:2048], self.PS[3][:, 0:1024]),
                   r=[PR[3], "gbc"], w=["gbc"])
            for fc in range(4):
                pb = PS[fc % 2]
                pbr = PR[fc % 2]
                for tq in range(4):
                    off = (tq % 2) * 512
                    for dc in range(ND):
                        sch.Tn(lambda e, fc=fc, tq=tq, dc=dc, kb=kb, pb=pb, off=off: e.matmul(
                            pb[:, off:off + 512],
                            kbf[kb][:, dc * 512 + fc * 128:dc * 512 + fc * 128 + 128],
                            self.xnT[:, dc * S + tq * 512:dc * S + tq * 512 + 512],
                            start=(dc == 0), stop=(dc == ND - 1)),
                            r=[f"kbf{kb}", pbr] + xnTr, w=[pbr])
                    sch.V(lambda e, tq=tq, pb=pb, off=off: e.scalar_tensor_tensor(
                        sgs[:, tq * 512:(tq + 1) * 512], pb[:, off:off + 512], 0.0,
                        gbc[:, tq * 512:(tq + 1) * 512], op0=ALU.max, op1=ALU.mult),
                        r=[pbr, "gbc", "sgs"], w=["sgs"])
                dv = ds["sg_d"][e_i, fc, :, :, :]
                sch.Y(lambda e, dv=dv: e.dma_start(
                    dv.rearrange("a p c -> p a c"),
                    sgs[:].rearrange("p (a c) -> p a c", a=NT)), r=["sgs"], w=["sgd"])

        for tcb in range(4):
            for e_i in range(NE):
                for fc in range(4):
                    sb = (e_i * 4 + fc) % 2
                    src = ds["sg_d"][e_i, fc, tcb * 4:(tcb + 1) * 4, :, :]
                    sch.Y(lambda e, src=src, sb=sb: e.dma_start(
                        sgt[sb][:].rearrange("p (a c) -> p a c", a=4),
                        src.rearrange("a p c -> p a c")),
                        r=[f"sgt{sb}"], w=[f"sgt{sb}"])
                    vsrc = ds["val_d"][e_i, fc * 128:(fc + 1) * 128, :]
                    sch.Y(lambda e, vsrc=vsrc, sb=sb: e.dma_start(vbt[sb][:], vsrc),
                          r=[f"vbt{sb}"], w=[f"vbt{sb}"])
                    first = (e_i == 0 and fc == 0)
                    last = (e_i == NE - 1 and fc == 3)
                    for tcl in range(4):
                        for (c0, c1) in ((0, 512), (512, 1024)):
                            sch.Tn(lambda e, tcl=tcl, c0=c0, c1=c1, sb=sb, first=first,
                                   last=last: e.matmul(
                                PS[tcl][:, c0:c1],
                                sgt[sb][:, tcl * 128:(tcl + 1) * 128],
                                vbt[sb][:, c0:c1],
                                start=first, stop=last),
                                r=[f"sgt{sb}", f"vbt{sb}", PR[tcl]], w=[PR[tcl]])
            for tcl in range(4):
                tc = tcb * 4 + tcl
                sch.V(lambda e, tc=tc, tcl=tcl: e.tensor_add(
                    self.x_sb[:, tc * D:(tc + 1) * D], self.x_sb[:, tc * D:(tc + 1) * D],
                    PS[tcl][:, 0:1024]), r=[PR[tcl], self.XT[tc]], w=[self.XT[tc]])

    def emit_final(self):
        sch = self.sch
        if DBG:
            sch.Y(lambda e: e.dma_start(self.ds["dbg_d"][:, :], self.dbg[:]), r=["dbg"], w=["dbgd"])
        tok = sch.Y(lambda e: e.dma_start(
            self.out.rearrange("(a p) c -> p a c", p=128),
            self.x_sb[:].rearrange("p (a c) -> p a c", a=NT)),
            r=self.XT, w=["outhbm"])
        sch.final_wait("Y", tok)


def build_program():
    nc = bass.Bass("TRN2", target_bir_lowering=False)
    di = {}

    def dram_in(name, shape, dt=F32):
        di[name] = nc.dram_tensor(name, shape, dt, kind="ExternalInput").ap()

    dram_in("x", [S, D])
    dram_in("wqT0", [D, D]); dram_in("wkT0", [D, DK]); dram_in("wvT0", [D, DK]); dram_in("woT0", [D, D])
    dram_in("sm_scale", [128, 1])
    dram_in("w1T", [D, DH]); dram_in("convp", [128, 128]); dram_in("w2T", [DH, D])
    dram_in("wqT1", [D, D]); dram_in("wkT1", [D, D]); dram_in("wvT1", [D, D]); dram_in("wpT1", [D, D])
    dram_in("selT", [D, NE]); dram_in("keys", [NE, D, EF]); dram_in("values", [NE, EF, D])
    dram_in("ident_b", [128, 128], BF16)
    dram_in("mask_st", [128, 128]); dram_in("mask_ts", [128, 128])
    dram_in("sel1h", [32, NE * 128], BF16)
    out = nc.dram_tensor("out", [S, D], F32, kind="ExternalOutput").ap()

    ds = {}

    def dram_s(name, shape, dt):
        kind = "ExternalOutput" if DBG else "Internal"
        ds[name] = nc.dram_tensor(name, shape, dt, kind=kind).ap()

    dram_s("w1q_d", [32, ND, 128, 128], BF16)
    dram_s("w2q_d", [32, 128, D], BF16)
    dram_s("xq_d", [S, D], BF16)
    dram_s("xn_d", [S, D], BF16)
    dram_s("sg_d", [NE, 4, NT, 128, 128], BF16)
    dram_s("val_d", [NE, EF, D], BF16)
    dram_s("row_d", [64, 128], BF16)
    dram_s("oT_d", [8, 128, S], BF16)
    dram_s("dbg_d", [128, 64], F32)

    with ExitStack() as es:
        k = Kern(nc, es, di, ds, out)
        k.emit_loads()
        if STAGE >= 2:
            k.emit_wprep()
        with ExitStack() as esA:
            TA = lambda n, s, d=F32: esA.enter_context(nc.sbuf_tensor(n, s, d))
            k.xnT = TA("xnT", [128, ND * S], BF16)
            k.emit_ln(0)
            if STAGE >= 3:
                k.alloc_attn(TA)
                k.emit_attn(0)
        if STAGE >= 4:
            k.sch.fence()
            with ExitStack() as esB:
                TB = lambda n, s, d=F32: esB.enter_context(nc.sbuf_tensor(n, s, d))
                k.emit_ffn(TB)
        if STAGE >= 5:
            k.sch.fence()
            with ExitStack() as esC:
                TC = lambda n, s, d=F32: esC.enter_context(nc.sbuf_tensor(n, s, d))
                k.xnT = TC("xnTL1", [128, ND * S], BF16)
                k.emit_ln(1)
                with ExitStack() as esC2:
                    TC2 = lambda n, s, d=F32: esC2.enter_context(nc.sbuf_tensor(n, s, d))
                    k.alloc_attn(TC2, sfx="L1")
                    k.emit_attn(1)
                if STAGE >= 6:
                    k.sch.fence()
                    with ExitStack() as esD:
                        TD = lambda n, s, d=F32: esD.enter_context(nc.sbuf_tensor(n, s, d))
                        k.emit_moe(TD)
        k.emit_final()
        k.sch.finalize(nc, es)
    return nc


def _host_prep(inputs):
    f = np.ascontiguousarray
    pre = {}
    pre["wqT0"] = f(np.asarray(inputs["L0_attn_w_q"]).T)
    pre["wkT0"] = f(np.asarray(inputs["L0_attn_w_k"]).T)
    pre["wvT0"] = f(np.asarray(inputs["L0_attn_w_v"]).T)
    pre["woT0"] = f(np.asarray(inputs["L0_attn_w_out"]).T)
    pre["sm_scale"] = np.broadcast_to(
        np.asarray(inputs["L0_attn_softmax_scale"]).reshape(1, 1), (128, 1)).copy()
    pre["w1T"] = f(np.asarray(inputs["L0_ffn_w1"]).T)
    cw = np.asarray(inputs["L0_ffn_conv_w"]).reshape(DH, 3)
    cb = np.asarray(inputs["L0_ffn_conv_b"]).reshape(DH, 1)
    cp = np.concatenate([cw, cb], axis=1).reshape(32, 128, 4)
    pre["convp"] = f(cp.transpose(1, 0, 2).reshape(128, 128))
    pre["w2T"] = f(np.asarray(inputs["L0_ffn_w2"]).T)
    pre["wqT1"] = f(np.asarray(inputs["L1_attn_W_query"]).T)
    pre["wkT1"] = f(np.asarray(inputs["L1_attn_W_key"]).T)
    pre["wvT1"] = f(np.asarray(inputs["L1_attn_W_value"]).T)
    pre["wpT1"] = f(np.asarray(inputs["L1_attn_W_proj"]).T)
    pre["selT"] = f(np.asarray(inputs["L1_ffn_expert_sel"]).T)
    pre["keys"] = f(np.asarray(inputs["L1_ffn_keys"]))
    pre["values"] = f(np.asarray(inputs["L1_ffn_values"]))
    pre["ident_b"] = np.eye(128, dtype=ml_dtypes.bfloat16)
    m = np.zeros((128, 128), np.float32)
    m[np.triu_indices(128, 1)] = NEG_BIG
    pre["mask_st"] = m
    pre["mask_ts"] = f(m.T)
    s1 = np.zeros((32, NE, 128), dtype=ml_dtypes.bfloat16)
    for e_ in range(NE):
        s1[e_, e_, :] = 1
    pre["sel1h"] = f(s1.transpose(1, 0, 2).reshape(32, NE * 128)) if False else f(
        np.ascontiguousarray(np.transpose(s1, (1, 0, 2))).reshape(32, NE * 128))
    return pre


def kernel(**inputs):
    pre = _host_prep(inputs)
    if "nc" not in _CACHE:
        _CACHE["nc"] = build_program()
    nc = _CACHE["nc"]
    x = np.ascontiguousarray(np.asarray(inputs["L0_attn_x"], dtype=np.float32))
    in_maps = []
    for c in range(8):
        m = dict(pre)
        m["x"] = np.ascontiguousarray(x[c])
        in_maps.append(m)
    res = run_bass_kernel_spmd(nc, in_maps, core_ids=list(range(8)))
    return np.stack([r["out"] for r in res.results], axis=0)
